# version 01 (final)
# baseline (speedup 1.0000x reference)
import contextlib
import numpy as np
import ml_dtypes
import concourse.bass as bass
import concourse.mybir as mybir
from concourse.bass_utils import run_bass_kernel_spmd

F32 = mybir.dt.float32
BF16 = mybir.dt.bfloat16
AF = mybir.ActivationFunctionType
ALU = mybir.AluOpType
AXX = mybir.AxisListType.X

N_CORES = 8
B = 32
T = 2048
D = 1024
SPC = B // N_CORES
H = 16
DH = 64
SCALE = DH ** -0.5
EPS = 1e-5

_STAGE = "full"
BF = ml_dtypes.bfloat16



def _cols(v, p=128):
    v = np.asarray(v, np.float32)
    return np.ascontiguousarray(v.reshape(v.size // p, p).T)


def _butterfly_dense(w_stage, stage, transposed, layout):
    s2 = 1 << stage
    out = np.zeros((32, 128, 128), np.float32)
    idx = np.arange(2048)
    i_feat = (idx // s2) * 2 * s2 + (idx % s2)
    j_feat = i_feat + s2
    w = np.asarray(w_stage, np.float32)

    def gp(hh):
        if layout == "L1":
            return hh >> 7, hh & 127
        return hh & 31, ((hh >> 5) & 3) * 32 + (hh >> 7)

    ga, pa = gp(i_feat)
    gb, pb = gp(j_feat)
    for po, pi, val in [(pa, pa, w[:, 0, 0]), (pa, pb, w[:, 0, 1]),
                        (pb, pa, w[:, 1, 0]), (pb, pb, w[:, 1, 1])]:
        if transposed:
            out[ga, pi, po] = val
        else:
            out[ga, po, pi] = val
    return out


def _prep_shared(inp):
    d = {}
    f32 = lambda a: np.ascontiguousarray(np.asarray(a, np.float32))
    d["qkvwT"] = f32(np.asarray(inp["L0_attn_a_qkv_w"]).T)
    d["projwT"] = f32(np.asarray(inp["L0_attn_a_proj_w"]).T)
    d["pbias"] = _cols(inp["L0_attn_a_proj_b"])
    qs = np.tile(np.asarray(inp["L0_attn_a_q_norm_w"], np.float32), 2)
    qb = np.tile(np.asarray(inp["L0_attn_a_q_norm_b"], np.float32), 2)
    ks = np.tile(np.asarray(inp["L0_attn_a_k_norm_w"], np.float32), 2)
    kb = np.tile(np.asarray(inp["L0_attn_a_k_norm_b"], np.float32), 2)
    d["qkpar"] = np.ascontiguousarray(np.stack([qs, qb, ks, kb], 1))
    d["onescol"] = np.ones((128, 1), np.float32)
    d["onesrow"] = np.ones((1, 128), np.float32)
    d["epscol"] = np.full((128, 1), EPS, np.float32)
    hsp = np.zeros((128, 2), np.float32)
    hsp[0:64, 0] = 1.0
    hsp[64:128, 1] = 1.0
    d["hsplit"] = hsp
    d["hsplitT"] = np.ascontiguousarray(hsp.T)
    d["ident"] = np.eye(128, dtype=np.float32)
    wet = np.asarray(inp["L0_ffn_W_exp"], np.float32).T
    bex = np.asarray(inp["L0_ffn_b_exp"], np.float32)[:, None]
    aug = np.zeros((4096, 1032), np.float32)
    aug[:, 0:1024] = wet
    aug[:, 1024:1025] = bex
    d["WexpT0"] = np.ascontiguousarray(aug)
    d["Wcon0"] = f32(inp["L0_ffn_W_con"])
    d["upbias"] = _cols(inp["L0_ffn_up_bias"])
    d["bcon"] = _cols(inp["L0_ffn_b_con"])
    d["dbias"] = _cols(inp["L0_ffn_down_bias"])
    uw = np.asarray(inp["L0_ffn_up_weights"], np.float32)
    dw = np.asarray(inp["L0_ffn_down_weights"], np.float32)
    d["bfly_ul"] = np.ascontiguousarray(np.stack(
        [_butterfly_dense(uw[s], s, s == 6, "L1") for s in range(7)]))
    d["bfly_uh"] = np.ascontiguousarray(np.stack(
        [_butterfly_dense(uw[s], s, s == 11, "L2") for s in range(7, 12)]))
    d["bfly_dl"] = np.ascontiguousarray(np.stack(
        [_butterfly_dense(dw[s], s, s != 0, "L1") for s in range(7)]))
    d["bfly_dh"] = np.ascontiguousarray(np.stack(
        [_butterfly_dense(dw[s], s, s != 7, "L2") for s in range(7, 12)]))
    d["WinT"] = f32(np.asarray(inp["L0_attn_b_W_in"]).T)
    d["WoutT"] = f32(np.asarray(inp["L0_attn_b_W_out"]).T)
    cw = np.asarray(inp["L0_attn_b_conv_w"], np.float32).reshape(18, 128, 4)
    d["convw"] = np.ascontiguousarray(cw.transpose(1, 2, 0).reshape(128, 72))
    d["convb"] = _cols(inp["L0_attn_b_conv_b"])
    d["dtbias4"] = np.ascontiguousarray(
        np.tile(np.asarray(inp["L0_attn_b_dt_bias"], np.float32), 4)[:, None])
    d["alog4"] = np.ascontiguousarray(
        np.tile(np.asarray(inp["L0_attn_b_A_log"], np.float32), 4)[:, None])
    d["ddcol"] = np.ascontiguousarray(
        np.repeat(np.asarray(inp["L0_attn_b_D"], np.float32), 64).reshape(16, 128).T)
    d["normw"] = _cols(inp["L0_attn_b_norm_w"])
    p = np.arange(128)[:, None]
    f = np.arange(256)[None, :]
    d["maskA"] = np.ascontiguousarray((f >= p).astype(np.float32))
    d["maskB"] = np.ascontiguousarray((f >= (p + 128)).astype(np.float32))
    return d


def _prep_core(inp, core):
    x = np.asarray(inp["L0_attn_a_x"][core * SPC:(core + 1) * SPC], np.float32)
    return {"xcore": np.ascontiguousarray(x.transpose(0, 2, 1))}



ENGINES = ("tensor", "vector", "scalar", "gpsimd", "sync")
DMA_QUEUES = ("sync", "gpsimd", "scalar")
N_DMA_SEMS = 8


class _Entry:
    __slots__ = ("fn", "eng", "pos", "flag", "snap", "dma_sem", "dma_cum")

    def __init__(self, fn, eng, pos, snap):
        self.fn = fn
        self.eng = eng
        self.pos = pos
        self.flag = False
        self.snap = snap
        self.dma_sem = None
        self.dma_cum = 0


class _Buf:
    __slots__ = ("writer", "readers")

    def __init__(self):
        self.writer = None
        self.readers = {}


class Rec:
    def __init__(self):
        self.items = {e: [] for e in ENGINES}
        self.npos = {e: 0 for e in ENGINES}
        self.clock = {e: {} for e in ENGINES}
        self.bufs = {}
        self.dma_rr = {q: 0 for q in DMA_QUEUES}
        self.dma_cum = {q: [0] * N_DMA_SEMS for q in DMA_QUEUES}

    def buf(self, key):
        b = self.bufs.get(key)
        if b is None:
            b = self.bufs[key] = _Buf()
        return b

    def _need(self, eng, ent):
        if ent is None:
            return
        if ent.dma_sem is not None:
            key = ("dma",) + ent.dma_sem
            q, si = ent.dma_sem
            if self.clock[eng].get(key, 0) >= ent.dma_cum:
                return
            val = self.dma_cum[q][si]
            self.items[eng].append(("dwait", key, val))
            self.clock[eng][key] = val
        else:
            if ent.eng == eng:
                return
            key = ("e", ent.eng)
            if self.clock[eng].get(key, -1) >= ent.pos:
                return
            ent.flag = True
            self.items[eng].append(("wait", ent))
            self.clock[eng][key] = ent.pos
        for k, v in ent.snap.items():
            if self.clock[eng].get(k, -1) < v:
                self.clock[eng][k] = v

    def emit(self, eng, fn, reads=(), writes=(), dma=False, queue=None):
        for r in reads:
            self._need(eng, self.buf(r).writer)
        for w in writes:
            b = self.buf(w)
            self._need(eng, b.writer)
            for rd in list(b.readers.values()):
                self._need(eng, rd)
        pos = self.npos[eng]
        self.npos[eng] = pos + 1
        snap = dict(self.clock[eng])
        snap[("e", eng)] = pos
        ent = _Entry(fn, eng, pos, snap)
        if dma:
            q = queue or eng
            si = self.dma_rr[q] % N_DMA_SEMS
            self.dma_rr[q] += 1
            self.dma_cum[q][si] += 16
            ent.dma_sem = (q, si)
            ent.dma_cum = self.dma_cum[q][si]
            ent.snap = dict(snap)
            ent.snap[("dma", q, si)] = ent.dma_cum
        self.items[eng].append(ent)
        for r in reads:
            self.buf(r).readers[eng] = ent
        for w in writes:
            b = self.buf(w)
            b.writer = ent
            b.readers = {}
        return ent

    def barrier(self):
        lasts = {}
        for e in ENGINES:
            for it in reversed(self.items[e]):
                if isinstance(it, _Entry):
                    lasts[e] = it
                    break
        for e in ENGINES:
            for pe, ent in lasts.items():
                if pe != e:
                    self._need(e, ent)
            for q in DMA_QUEUES:
                for si in range(N_DMA_SEMS):
                    cum = self.dma_cum[q][si]
                    key = ("dma", q, si)
                    if cum > 0 and self.clock[e].get(key, 0) < cum:
                        self.items[e].append(("dwait", key, cum))
                        self.clock[e][key] = cum

    def finalize_and_replay(self, nc):
        val_of = {e: {} for e in ENGINES}
        for e in ENGINES:
            cnt = 0
            for it in self.items[e]:
                if isinstance(it, _Entry) and it.flag and it.dma_sem is None:
                    cnt += 1
                    val_of[e][it.pos] = cnt
        with contextlib.ExitStack() as ctx:
            esem = {e: ctx.enter_context(nc.semaphore(f"es_{e}")) for e in ENGINES}
            dsem = {q: [ctx.enter_context(nc.semaphore(f"ds_{q}{i}"))
                        for i in range(N_DMA_SEMS)] for q in DMA_QUEUES}
            block = ctx.enter_context(nc.Block())

            def make_body(e):
                def body(eng):
                    for it in self.items[e]:
                        if isinstance(it, _Entry):
                            inst = it.fn(eng)
                            if it.dma_sem is not None:
                                q, si = it.dma_sem
                                inst.then_inc(dsem[q][si], 16)
                            elif it.flag:
                                inst.then_inc(esem[e], 1)
                        elif it[0] == "wait":
                            ent = it[1]
                            eng.wait_ge(esem[ent.eng], val_of[ent.eng][ent.pos])
                        else:
                            _, key, val = it
                            eng.wait_ge(dsem[key[1]][key[2]], val)
                return body

            for e in ENGINES:
                getattr(block, e)(make_body(e))


def _xview(dram, tok0, ntok):
    return dram[:, tok0:tok0 + ntok].rearrange("(ct p) t -> p ct t", p=128)


def _c3(sb_ap, ct=8):
    return sb_ap.rearrange("p (a b) -> p a b", a=ct)



class KB:
    def __init__(self, stage):
        self.stage = stage
        nc = bass.Bass("TRN2", target_bir_lowering=False)
        self.nc = nc
        self.R = Rec()
        self.ctx = contextlib.ExitStack()
        dr = lambda n, s, dt, k: nc.dram_tensor(n, s, dt, kind=k).ap()
        self.xcore = dr("xcore", [SPC, D, T], F32, "ExternalInput")
        self.ycore = dr("ycore", [SPC, D, T], F32, "ExternalOutput")
        self.hin = {}
        for n, s in [("qkvwT", [D, 3072]), ("projwT", [D, D]), ("pbias", [128, 8]),
                     ("qkpar", [128, 4]), ("onescol", [128, 1]), ("onesrow", [1, 128]),
                     ("epscol", [128, 1]), ("hsplit", [128, 2]), ("hsplitT", [2, 128]),
                     ("ident", [128, 128]),
                     ("WexpT0", [4096, 1032]), ("Wcon0", [4096, D]),
                     ("upbias", [128, 32]), ("bcon", [128, 8]), ("dbias", [128, 8]),
                     ("bfly_ul", [7, 32, 128, 128]), ("bfly_uh", [5, 32, 128, 128]),
                     ("bfly_dl", [7, 32, 128, 128]), ("bfly_dh", [5, 32, 128, 128]),
                     ("WinT", [D, 4384]), ("WoutT", [2048, D]),
                     ("convw", [128, 72]), ("convb", [128, 18]),
                     ("dtbias4", [128, 1]), ("alog4", [128, 1]), ("ddcol", [128, 16]),
                     ("normw", [128, 16]), ("maskA", [128, 256]), ("maskB", [128, 256])]:
            self.hin[n] = dr(n, s, F32, "ExternalInput")
        self.XD = [dr(f"XD{i}", [D, T], F32, "Internal") for i in range(2)]
        self.QKVBF = dr("QKVBF", [D, 3072], BF16, "Internal")
        self.PROJBF = dr("PROJBF", [D, D], BF16, "Internal")
        self.WINBF = dr("WINBF", [D, 4384], BF16, "Internal")
        self.WOUTBF = dr("WOUTBF", [2048, D], BF16, "Internal")
        self.WEXPBF = dr("WEXPBF", [D, 4096], BF16, "Internal")
        self.WEXPT_BF = dr("WEXPT_BF", [4096, D], BF16, "Internal")
        self.WCONBF = dr("WCONBF", [4096, D], BF16, "Internal")
        self.EX1 = dr("EX1", [4096, 1032], F32, "Internal")
        self.EX2 = dr("EX2", [4096, 1032], F32, "Internal")
        self.RP = dr("RP", [1, 16384], F32, "Internal")
        self.RPO = dr("RPO", [1, 16384], F32, "Internal")
        self.CSD = dr("CSD", [32, T], F32, "Internal")
        self.CLD = dr("CLD", [8, 32], F32, "Internal")
        self.ZD = dr("ZD", [2048, T], BF16, "Internal")
        self.CVD = dr("CVD", [128, 54], BF16, "Internal")
        sb = lambda n, s, dt: self.ctx.enter_context(nc.sbuf_tensor(n, s, dt))
        self.W1 = sb("W1", [128, 35072], BF16)
        self.W2 = sb("W2", [128, 32768], BF16)
        self.A = sb("A", [128, 18488], BF16)
        self.F = sb("F", [128, 8064], F32)
        self.RR = sb("RR", [128, 1024], F32)
        self.C = sb("C", [128, 580], F32)
        self.CB = sb("CB", [128, 648], BF16)
        self.PF = self.ctx.enter_context(nc.psum_tensor("PF", [128, 3072], F32))
        self.PB = self.ctx.enter_context(nc.psum_tensor("PB", [128, 2048], BF16))

    def t_mm(self, out, lhsT, rhs, start, stop, reads, writes, tp=None):
        self.R.emit("tensor",
                    lambda e, o=out, l=lhsT, r=rhs, s1=start, s2=stop, tp=tp:
                    e.matmul(o, l, r, start=s1, stop=s2, tile_position=tp),
                    reads=reads, writes=writes)

    def t_tr(self, out, in_, ident, reads, writes):
        self.R.emit("tensor", lambda e, o=out, i=in_, d=ident: e.transpose(o, i, d),
                    reads=reads, writes=writes)

    def dma(self, queue, out, in_, reads=(), writes=(), **kw):
        self.R.emit(queue,
                    lambda e, o=out, i=in_, kw=kw: e.dma_start(out=o, in_=i, **kw),
                    reads=reads, writes=writes, dma=True, queue=queue)

    def ve(self, name, *args, reads=(), writes=(), eng="vector", **kw):
        self.R.emit(eng, lambda e, n=name, a=args, kw=kw: getattr(e, n)(*a, **kw),
                    reads=reads, writes=writes)

    def act(self, out, in_, func, reads, writes, bias=0.0, scale=1.0):
        self.R.emit("scalar",
                    lambda e, o=out, i=in_, f=func, b=bias, s=scale:
                    e.activation(out=o, in_=i, func=f, bias=b, scale=s),
                    reads=reads, writes=writes)

    def pf(self, bank, cols=512, off=0, rows=(0, 128)):
        return self.PF[rows[0]:rows[1], bank * 512 + off: bank * 512 + off + cols]

    def pb16(self, bank, cols=1024, off=0):
        return self.PB[:, bank * 1024 + off: bank * 1024 + off + cols]

    def _load_consts(self):
        dma, C, CB, h = self.dma, self.C, self.CB, self.hin
        for view, src in [
            (C[:, 0:1], h["onescol"]), (C[:, 1:2], h["epscol"]),
            (C[:, 2:6], h["qkpar"]), (C[:, 6:14], h["pbias"]),
            (C[:, 14:86], h["convw"]), (C[:, 86:104], h["convb"]),
            (C[:, 104:105], h["dtbias4"]), (C[:, 105:106], h["alog4"]),
            (C[:, 106:122], h["ddcol"]), (C[:, 122:138], h["normw"]),
            (C[:, 138:170], h["upbias"]), (C[:, 170:178], h["bcon"]),
            (C[:, 178:186], h["dbias"]), (C[:, 186:188], h["hsplit"]),
            (C[:, 192:320], h["ident"]), (C[0:2, 324:452], h["hsplitT"]),
            (C[0:1, 452:580], h["onesrow"]),
            (C[64:66, 324:452], h["hsplitT"]), (C[64:65, 452:580], h["onesrow"]),
            (C[32:33, 452:580], h["onesrow"]),
        ]:
            dma("sync", view, src, writes=["C"])
        for view, src in [
            (CB[:, 0:128], h["ident"]), (CB[:, 128:129], h["onescol"]),
            (CB[:, 130:132], h["hsplit"]), (CB[:, 132:388], h["maskA"]),
            (CB[:, 388:644], h["maskB"]),
        ]:
            dma("gpsimd", view, src, writes=["CB"])
        self.ve("tensor_scalar_mul", C[:, 320:321], C[:, 2:3], SCALE,
                reads=["C"], writes=["Cd"])
        self.act(C[:, 322:323], C[:, 105:106], AF.Exp, reads=["C"], writes=["Cd"])
        self.ve("tensor_scalar_mul", C[:, 322:323], C[:, 322:323], -1.0,
                reads=["Cd"], writes=["Cd"])
        self.ve("tensor_scalar_mul", C[:, 321:322], C[:, 3:4], SCALE,
                reads=["C"], writes=["Cd"])

    def _setup_weights_bf16(self):
        dma = self.dma
        specs = [(self.hin["qkvwT"], self.QKVBF, D, 3072, False),
                 (self.hin["projwT"], self.PROJBF, D, D, False),
                 (self.hin["WinT"], self.WINBF, D, 4384, False),
                 (self.hin["WoutT"], self.WOUTBF, 2048, D, True)]
        for src, dst, rows, cols, scale in specs:
            for kt in range(rows // 128):
                slot = kt % 3
                fview = self.F[:, slot * 2048:slot * 2048 + 2048]
                bview = self.W1[:, slot * 4608:slot * 4608 + 4608]
                for ci, c0 in enumerate(range(0, cols, 2048)):
                    cw = min(2048, cols - c0)
                    fv = fview[:, 0:cw]
                    bv = bview[:, ci * 2048:ci * 2048 + cw]
                    kf, kb = f"sF{slot}", f"sB{slot}_{ci}"
                    dma("sync", fv, src[kt * 128:(kt + 1) * 128, c0:c0 + cw],
                        writes=[kf])
                    if not scale:
                        self.ve("tensor_copy", bv, fv, reads=[kf], writes=[kb])
                    else:
                        self.ve("tensor_scalar_mul", bv, fv,
                                self.C[:, 122 + kt:123 + kt],
                                reads=[kf, "C"], writes=[kb])
                    dma("scalar", dst[kt * 128:(kt + 1) * 128, c0:c0 + cw], bv,
                        reads=[kb], writes=[dst.tensor.name])
        self.R.barrier()

    def _setup_butterfly(self):
        dma, h = self.dma, self.hin

        def chain_and_apply(fstack, nst, up_style, src, dst, cols, skey, dkey):
            ks = list(range(nst))
            direct = ks[-1] if up_style else ks[0]
            mm_order = ks[-2::-1] if up_style else ks[1:]
            for g in range(32):
                sl = g % 2
                fac = self.F[:, sl * 256:sl * 256 + 128]
                cur = self.F[:, sl * 256 + 128:sl * 256 + 256]
                wt = self.F[:, 512 + sl * 1088:512 + sl * 1088 + 1088]
                wo = self.F[:, 2688 + sl * 1088:2688 + sl * 1088 + 1088]
                kfa, kcu, kwt, kwo = (f"bf{sl}", f"bc{sl}", f"bw{sl}", f"bo{sl}")
                psA = self.pf(sl, 128)
                kps = f"pfb{sl}"
                dma("sync", cur, fstack[direct, g], writes=[kcu])
                for k in mm_order:
                    dma("sync", fac, fstack[k, g], writes=[kfa])
                    self.t_mm(psA, fac, cur, True, True, reads=[kfa, kcu],
                              writes=[kps])
                    self.ve("tensor_copy", cur, psA, reads=[kps], writes=[kcu])
                dma("sync", wt[:, 0:cols], src[g * 128:(g + 1) * 128, 0:cols],
                    reads=[skey], writes=[kwt])
                psB = self.pf(2 + sl, 128)
                kpb = f"pfb{2 + sl}"
                for c0 in range(0, cols, 128):
                    cw = min(128, cols - c0)
                    self.t_mm(psB[:, 0:cw], cur, wt[:, c0:c0 + cw], True, True,
                              reads=[kcu, kwt], writes=[kpb])
                    self.ve("tensor_copy", wo[:, c0:c0 + cw], psB[:, 0:cw],
                            reads=[kpb], writes=[kwo])
                dma("scalar", dst[g * 128:(g + 1) * 128, 0:cols], wo[:, 0:cols],
                    reads=[kwo], writes=[dkey])
            self.R.barrier()

        def exchange(src, dst, fwd, cols, skey, dkey):
            if fwd:
                view = src[:, 0:cols].rearrange("(g b v) c -> v b g c",
                                                g=32, b=4, v=32)
                for v in range(32):
                    dma("sync", dst[v * 128:(v + 1) * 128, 0:cols]
                        .rearrange("(b g) c -> b g c", b=4), view[v],
                        reads=[skey], writes=[dkey])
            else:
                view = dst[:, 0:cols].rearrange("(g b v) c -> v b g c",
                                                g=32, b=4, v=32)
                for v in range(32):
                    dma("sync", view[v],
                        src[v * 128:(v + 1) * 128, 0:cols]
                        .rearrange("(b g) c -> b g c", b=4),
                        reads=[skey], writes=[dkey])
            self.R.barrier()

        chain_and_apply(h["bfly_ul"], 7, True, h["WexpT0"], self.EX1, 1032,
                        "WexpT0", "EX1a")
        exchange(self.EX1, self.EX2, True, 1032, "EX1a", "EX2a")
        chain_and_apply(h["bfly_uh"], 5, True, self.EX2, self.EX1, 1032,
                        "EX2a", "EX1b")
        exchange(self.EX2, self.EX1, False, 1032, "EX1b", "EX2b")
        for g in range(32):
            sl = g % 2
            fv = self.F[:, sl * 1088:sl * 1088 + 1024]
            bv = self.W1[:, sl * 1024:sl * 1024 + 1024]
            kf, kb = f"cF{sl}", f"cB{sl}"
            dma("sync", fv, self.EX2[g * 128:(g + 1) * 128, 0:1024],
                reads=["EX2b"], writes=[kf])
            self.ve("tensor_copy", bv, fv, reads=[kf], writes=[kb])
            dma("scalar", self.WEXPT_BF[g * 128:(g + 1) * 128, :], bv,
                reads=[kb], writes=["WEXPT_BF"])
        for t in range(32):
            dma("sync", self.F[:, 4864 + t:4865 + t],
                self.EX2[t * 128:(t + 1) * 128, 1024:1025],
                reads=["EX2b"], writes=["bxf"],
                allow_slow_non_contiguous=True)
        self.R.barrier()
        for ct in range(8):
            bv = self.W1[:, 2048:2048 + 4096]
            self.R.emit("scalar",
                        lambda e, o=bv, i=self.WEXPT_BF[:, ct * 128:(ct + 1) * 128]:
                        e.dma_start(out=o, in_=i, transpose=True),
                        reads=["WEXPT_BF"], writes=["cT"], dma=True, queue="scalar")
            dma("sync", self.WEXPBF[ct * 128:(ct + 1) * 128, :], bv,
                reads=["cT"], writes=["WEXPBF"])
        self.ve("tensor_add", self.C[:, 138:170], self.C[:, 138:170],
                self.F[:, 4864:4896], reads=["bxf", "C"], writes=["Cgb"])
        self.ve("tensor_add", self.C[:, 170:178], self.C[:, 170:178],
                self.C[:, 178:186], reads=["C"], writes=["Cb2"])
        exchange(h["Wcon0"], self.EX1, True, 1024, "Wcon0", "EX1c")
        chain_and_apply(h["bfly_dh"], 5, False, self.EX1, self.EX2, 1024,
                        "EX1c", "EX2c")
        exchange(self.EX2, self.EX1, False, 1024, "EX2c", "EX1d")
        chain_and_apply(h["bfly_dl"], 7, False, self.EX1, self.EX2, 1024,
                        "EX1d", "EX2d")
        for g in range(32):
            sl = g % 2
            fv = self.F[:, sl * 1088:sl * 1088 + 1024]
            bv = self.W1[:, sl * 1024:sl * 1024 + 1024]
            kf, kb = f"cF{sl}", f"cB{sl}"
            dma("sync", fv, self.EX2[g * 128:(g + 1) * 128, 0:1024],
                reads=["EX2d"], writes=[kf])
            self.ve("tensor_copy", bv, fv, reads=[kf], writes=[kb])
            dma("scalar", self.WCONBF[g * 128:(g + 1) * 128, :], bv,
                reads=[kb], writes=["WCONBF"])
        self.R.barrier()

    def _ln_chunk(self, xdram, xkey, ch, xn3, xn_key):
        sl = ch % 2
        F, C, CB, RR = self.F, self.C, self.CB, self.RR
        xc = F[:, sl * 2048:sl * 2048 + 2048]
        kxc = f"xc{sl}"
        xb = F[:, 4096:5120].bitcast(BF16)
        kxb = "xb"
        self.dma("sync", _c3(xc), _xview(xdram, ch * 256, 256), reads=[xkey],
                 writes=[kxc])
        self.ve("tensor_copy", xb, xc, reads=[kxc], writes=[kxb], eng="gpsimd")
        bs, bq = (4, 5) if sl == 0 else (2, 3)
        ps_s = self.pf(bs, 256, 0, rows=(0, 1))
        ps_q = self.pf(bq, 256, 0, rows=(0, 1))
        ks, kq = f"pfb{bs}", f"pfb{bq}"
        for ct in range(8):
            self.t_mm(ps_s, CB[:, 128:129], xb[:, ct * 256:(ct + 1) * 256],
                      ct == 0, ct == 7, reads=[kxb, "CB"], writes=[ks])
        self.ve("tensor_mul", xb, xb, xb, reads=[kxb], writes=[kxb], eng="gpsimd")
        for ct in range(8):
            self.t_mm(ps_q, CB[:, 128:129], xb[:, ct * 256:(ct + 1) * 256],
                      ct == 0, ct == 7, reads=[kxb, "CB"], writes=[kq])
        p0 = 32
        m = RR[p0:p0 + 1, 0:256]
        rstd = RR[p0:p0 + 1, 256:512]
        krow = "lnr"
        self.ve("tensor_scalar_mul", m, ps_s, 1.0 / 1024, reads=[ks], writes=[krow])
        tmpm = RR[p0:p0 + 1, 512:768]
        self.ve("tensor_mul", tmpm, m, m, reads=[krow], writes=[krow + "m"])
        self.ve("scalar_tensor_tensor", rstd, ps_q, 1.0 / 1024, tmpm,
                op0=ALU.mult, op1=ALU.subtract,
                reads=[kq, krow + "m"], writes=[krow + "r"])
        self.act(rstd, rstd, AF.Ln, bias=C[p0:p0 + 1, 1:2],
                 reads=[krow + "r", "C"], writes=[krow + "r"])
        self.act(rstd, rstd, AF.Exp, scale=-0.5,
                 reads=[krow + "r"], writes=[krow + "r"])
        pb = self.pf(sl, 512)
        kpb = f"pfb{sl}"
        orow = C[p0:p0 + 1, 452:580]
        self.t_mm(pb[:, 0:256], orow, m, True, True,
                  reads=[krow, "C"], writes=[kpb])
        self.t_mm(pb[:, 256:512], orow, rstd, True, True,
                  reads=[krow + "r", "C"], writes=[kpb])
        bce = self.F[:, 7168:7680]
        self.ve("tensor_copy", bce, pb, reads=[kpb], writes=["bce"])
        mB = bce[:, 0:256].unsqueeze(1).broadcast_to([128, 8, 256])
        rB = bce[:, 256:512].unsqueeze(1).broadcast_to([128, 8, 256])
        self.ve("tensor_tensor", xb.rearrange("p (a b) -> p a b", a=8),
                xc.rearrange("p (a b) -> p a b", a=8), mB, op=ALU.subtract,
                reads=[kxc, "bce"], writes=[kxb])
        self.ve("tensor_tensor", xn3,
                xb.rearrange("p (a b) -> p a b", a=8), rB, op=ALU.mult,
                reads=[kxb, "bce"], writes=[xn_key], eng="gpsimd")

    def _attn(self, s, xsrc, xskey, xdst, xdkey):
        R, dma, F, C, CB, RR, W1, W2, A = (self.R, self.dma, self.F, self.C,
                                           self.CB, self.RR, self.W1, self.W2,
                                           self.A)
        for kt in range(8):
            dma("sync", W1[:, kt * 3072:(kt + 1) * 3072],
                self.QKVBF[kt * 128:(kt + 1) * 128, :],
                reads=["QKVBF"], writes=[f"w1q{kt}"])
        for kt in range(8):
            dma("sync", W1[:, 24576 + kt * 1024:24576 + (kt + 1) * 1024],
                self.PROJBF[kt * 128:(kt + 1) * 128, :],
                reads=["PROJBF"], writes=[f"w1p{kt}"])
        for ch in range(8):
            self._ln_chunk(xsrc, xskey, ch,
                           A[:, ch * 2048:(ch + 1) * 2048]
                           .rearrange("p (a b) -> p a b", a=8), f"xn{ch}")
        R.barrier()

        def xn_rhs(kt, qc):
            base = (2 * qc) * 2048 + kt * 256
            v = A[:, base:base + 2304]
            return v.rearrange("p (a b) -> p a b", a=9)[:, 0:9:8, :]

        O = lambda hp: W2[:, hp * 2048:(hp + 1) * 2048]
        qp = W2[:, 16384:18432]
        kp = W2[:, 18432:20480]
        vt65 = W2[:, 20480:22560]
        t2k = W2[:, 22560:24608]
        pt = [W2[:, 24608 + i * 1024:24608 + (i + 1) * 1024] for i in range(2)]
        vraw = W2[:, 26656:28704]
        q2s = [W2[:, 28704 + i * 512:28704 + (i + 1) * 512] for i in range(2)]
        tmpb = [W2[:, 29728 + i * 512:29728 + (i + 1) * 512] for i in range(2)]
        rpA = self.RP[0:1, :].rearrange("q (h Q t) -> (q h) Q t", h=2, Q=4)
        rpP = self.RP[0:1, :].rearrange("q (hQ p a) -> (q p) hQ a",
                                        hQ=8, p=128, a=16)
        roA = self.RPO[0:1, 0:8192].rearrange("q (st p a) -> (q p) st a",
                                              st=4, p=128, a=16)
        roB = self.RPO[0:1, 8192:16384].rearrange("q (st p a) -> (q p) st a",
                                                  st=4, p=128, a=16)
        rvA = self.RPO[0:1, 0:8192].rearrange("q (st t) -> (q st) t", st=4)
        rvB = self.RPO[0:1, 8192:16384].rearrange("q (st t) -> (q st) t", st=4)

        for hp in range(8):
            kO = f"O{hp}"
            for qc in range(4):
                rk = [f"xn{2 * qc}", f"xn{2 * qc + 1}"]
                cols = {"q": hp * 128, "k": 1024 + hp * 128, "v": 2048 + hp * 128}
                banks = {"q": 0, "k": 1, "v": 2}
                for part in ("q", "k", "v"):
                    pbk = self.pf(banks[part])
                    kps = f"pfb{banks[part]}"
                    for kt in range(8):
                        lhs = W1[:, kt * 3072 + cols[part]:
                                 kt * 3072 + cols[part] + 128]
                        self.t_mm(pbk, lhs, xn_rhs(kt, qc), kt == 0, kt == 7,
                                  reads=[f"w1q{kt}"] + rk, writes=[kps])
                    if part == "q":
                        self.ve("tensor_copy", qp[:, qc * 512:(qc + 1) * 512],
                                pbk, reads=[kps], writes=["qp"])
                        self.ve("tensor_mul", q2s[0],
                                qp[:, qc * 512:(qc + 1) * 512],
                                qp[:, qc * 512:(qc + 1) * 512],
                                reads=["qp"], writes=["q20"], eng="gpsimd")
                        self.t_mm(self.pf(3, rows=(0, 2)), CB[:, 130:132],
                                  qp[:, qc * 512:(qc + 1) * 512], True, True,
                                  reads=["qp", "CB"], writes=["pfb3"])
                    elif part == "k":
                        self.ve("tensor_copy", kp[:, qc * 512:(qc + 1) * 512],
                                pbk, reads=[kps], writes=["kp"])
                        self.ve("tensor_mul", q2s[1],
                                kp[:, qc * 512:(qc + 1) * 512],
                                kp[:, qc * 512:(qc + 1) * 512],
                                reads=["kp"], writes=["q21"], eng="gpsimd")
                        self.t_mm(self.pf(4, rows=(0, 2)), CB[:, 130:132],
                                  q2s[0], True, True,
                                  reads=["q20", "CB"], writes=["pfb4"])
                        self.t_mm(self.pf(5, rows=(0, 2)), CB[:, 130:132],
                                  kp[:, qc * 512:(qc + 1) * 512], True, True,
                                  reads=["kp", "CB"], writes=["pfb5"])
                    else:
                        self.ve("tensor_copy", vraw[:, qc * 512:(qc + 1) * 512],
                                pbk, reads=[kps], writes=["vraw"])
                        self.t_mm(self.pf(0, rows=(0, 2)), CB[:, 130:132],
                                  q2s[1], True, True,
                                  reads=["q21", "CB"], writes=["pfb0"])
                p0 = 0 if qc % 2 == 0 else 32
                rv = RR[p0:p0 + 2, 0:1024]
                krr = f"rrs{qc % 2}"
                self.ve("tensor_copy", rv[:, 0:512], self.pf(3, rows=(0, 2)),
                        reads=["pfb3"], writes=[krr])
                self.ve("tensor_copy", rv[:, 512:1024], self.pf(4, rows=(0, 2)),
                        reads=["pfb4"], writes=[krr])
                dma("sync", rpA[0:2, 0:2, qc * 512:(qc + 1) * 512],
                    rv.rearrange("h (Q t) -> h Q t", Q=2),
                    reads=[krr], writes=["RPq"])
                self.ve("tensor_copy", rv[:, 0:512], self.pf(5, rows=(0, 2)),
                        reads=["pfb5"], writes=[krr])
                self.ve("tensor_copy", rv[:, 512:1024], self.pf(0, rows=(0, 2)),
                        reads=["pfb0"], writes=[krr])
                dma("sync", rpA[0:2, 2:4, qc * 512:(qc + 1) * 512],
                    rv.rearrange("h (Q t) -> h Q t", Q=2),
                    reads=[krr], writes=["RPk"])
            pk = F[:, 1536:1664]
            dma("sync", pk.rearrange("p (s a) -> p s a", a=16), rpP,
                reads=["RPq", "RPk"], writes=["pk"])
            pkv = pk.rearrange("p (s a) -> p s a", a=16)
            sums = pkv[:, 0:8:2, :]
            sqs = pkv[:, 1:8:2, :]
            m_ = F[:, 1664:1728]
            msq_ = F[:, 1728:1792]
            var_ = F[:, 1792:1856]
            b_ = F[:, 1856:1920]
            self.ve("tensor_scalar_mul", m_.rearrange("p (s a) -> p s a", a=16),
                    sums, 1.0 / 64, reads=["pk"], writes=["m_"])
            self.ve("tensor_mul", msq_, m_, m_, reads=["m_"], writes=["msq_"])
            self.ve("scalar_tensor_tensor",
                    var_.rearrange("p (s a) -> p s a", a=16), sqs, 1.0 / 64,
                    msq_.rearrange("p (s a) -> p s a", a=16),
                    op0=ALU.mult, op1=ALU.subtract,
                    reads=["pk", "msq_"], writes=["var_"])
            self.act(var_, var_, AF.Ln, bias=C[:, 1:2], reads=["var_", "C"],
                     writes=["var_"])
            self.act(var_, var_, AF.Exp, scale=-0.5, reads=["var_"],
                     writes=["var_"])
            self.ve("tensor_mul", b_, m_, var_, reads=["m_", "var_"],
                    writes=["b_"])
            self.ve("tensor_scalar_mul", b_, b_, -1.0, reads=["b_"], writes=["b_"])
            dma("sync", roA, var_.rearrange("p (s a) -> p s a", a=16),
                reads=["var_"], writes=["RPa"])
            dma("sync", roB, b_.rearrange("p (s a) -> p s a", a=16),
                reads=["b_"], writes=["RPb"])
            for qc in range(4):
                p0 = 0 if qc % 2 == 0 else 64
                ar = RR[p0:p0 + 2, 0:512]
                br = RR[p0:p0 + 2, 512:1024]
                hrow = C[p0:p0 + 2, 324:452]
                krr = f"rrs{qc % 2}"
                for isq, (st_sel, dst_pk, key_dst, t2dst) in enumerate(
                        [(0, qp, "qp", None), (1, kp, "kp", t2k)]):
                    dma("sync", ar, rvA[st_sel:4:2,
                                        qc * 512:(qc + 1) * 512],
                        reads=["RPa"], writes=[krr])
                    dma("sync", br, rvB[st_sel:4:2,
                                        qc * 512:(qc + 1) * 512],
                        reads=["RPb"], writes=[krr])
                    pbc = self.pf(1)
                    pbc2 = self.pf(2)
                    self.t_mm(pbc, hrow, ar, True, True,
                              reads=[krr, "C"], writes=["pfb1"])
                    self.t_mm(pbc2, hrow, br, True, True,
                              reads=[krr, "C"], writes=["pfb2"])
                    aB = F[:, 0:512]
                    bB = F[:, 512:1024]
                    self.ve("tensor_copy", aB, pbc, reads=["pfb1"], writes=["aB"])
                    self.ve("tensor_copy", bB, pbc2, reads=["pfb2"], writes=["bB"])
                    sl_ = qp[:, qc * 512:(qc + 1) * 512] if isq == 0 else \
                        kp[:, qc * 512:(qc + 1) * 512]
                    self.ve("tensor_tensor", tmpb[isq], sl_, aB, op=ALU.mult,
                            reads=[key_dst, "aB"], writes=[f"tm{isq}"])
                    if isq == 0:
                        self.ve("tensor_tensor", tmpb[0], tmpb[0], bB,
                                op=ALU.add, reads=["tm0", "bB"],
                                writes=["tm0"], eng="gpsimd")
                        self.ve("tensor_scalar", sl_, tmpb[0],
                                C[:, 320:321], C[:, 321:322],
                                op0=ALU.mult, op1=ALU.add,
                                reads=["tm0", "C", "Cd"], writes=["qp"],
                                eng="gpsimd")
                    else:
                        self.ve("tensor_tensor",
                                t2k[:, qc * 512:(qc + 1) * 512], tmpb[1], bB,
                                op=ALU.add, reads=["tm1", "bB"],
                                writes=["t2k"], eng="gpsimd")
            kmean = F[:, 1024:1025]
            nb2 = F[:, 1025:1026]
            self.ve("reduce_sum", kmean, t2k, axis=AXX, reads=["t2k"],
                    writes=["kmean"])
            self.ve("tensor_mul", nb2, kmean, C[:, 4:5], reads=["kmean", "C"],
                    writes=["nb2"])
            self.ve("tensor_scalar_mul", nb2, nb2, -1.0 / 2048, reads=["nb2"],
                    writes=["nb2"])
            for qc in range(4):
                self.ve("tensor_scalar", kp[:, qc * 512:(qc + 1) * 512],
                        t2k[:, qc * 512:(qc + 1) * 512],
                        C[:, 4:5], nb2, op0=ALU.mult, op1=ALU.add,
                        reads=["t2k", "nb2", "C"], writes=["kp"], eng="gpsimd")
            vt_dst = vt65.rearrange("p (k hh e) -> p k hh e", k=16, hh=2, e=65)
            for kb2 in range(16):
                bank = kb2 // 8
                self.t_tr(self.pb16(bank, 128, (kb2 % 8) * 128),
                          vraw[:, kb2 * 128:(kb2 + 1) * 128], CB[:, 0:128],
                          reads=["vraw", "CB"], writes=[f"pbb{bank}"])
                if kb2 % 8 == 7:
                    src = self.pb16(bank, 1024).rearrange(
                        "p (k hh d) -> p k hh d", k=8, hh=2)
                    self.ve("tensor_copy",
                            vt_dst[:, bank * 8:bank * 8 + 8, :, 0:64], src,
                            reads=[f"pbb{bank}"], writes=["vt65"])
            self.ve("memset", vt_dst[:, :, :, 64:65], 1.0, reads=[],
                    writes=["vt65o"])
            for qc in range(4):
                for kb2 in range(16):
                    ssl = kb2 % 2
                    sc = self.pf(ssl * 2, 1024)
                    kss = [f"pfb{ssl * 2}", f"pfb{ssl * 2 + 1}"]
                    self.t_mm(sc[:, 0:512],
                              kp[0:64, kb2 * 128:(kb2 + 1) * 128],
                              qp[0:64, qc * 512:(qc + 1) * 512],
                              True, True, reads=["kp", "qp"], writes=kss,
                              tp=(0, 0))
                    self.t_mm(sc[:, 512:1024],
                              kp[64:128, kb2 * 128:(kb2 + 1) * 128],
                              qp[64:128, qc * 512:(qc + 1) * 512],
                              True, True, reads=["kp", "qp"], writes=kss,
                              tp=(64, 0))
                    self.act(pt[ssl], sc, AF.Exp, reads=kss,
                             writes=[f"pt{ssl}"])
                    self.t_mm(self.pf(4, rows=(0, 65)),
                              vt65[:, kb2 * 130:kb2 * 130 + 65],
                              pt[ssl][:, 0:512], kb2 == 0, kb2 == 15,
                              reads=[f"pt{ssl}", "vt65", "vt65o"],
                              writes=["pfb4"])
                    self.t_mm(self.pf(5, rows=(0, 65)),
                              vt65[:, kb2 * 130 + 65:kb2 * 130 + 130],
                              pt[ssl][:, 512:1024], kb2 == 0, kb2 == 15,
                              reads=[f"pt{ssl}", "vt65", "vt65o"],
                              writes=["pfb5"])
                rrA = RR[0:1, 0:512]
                rrB = RR[64:65, 0:512]
                self.ve("reciprocal", rrA, self.pf(4, rows=(64, 65)),
                        reads=["pfb4"], writes=["rrs0"])
                self.ve("reciprocal", rrB, self.pf(5, rows=(64, 65)),
                        reads=["pfb5"], writes=["rrs1"])
                pbc = self.pf(0)
                self.t_mm(pbc[0:64, :], C[0:1, 452:516], rrA, True, True,
                          reads=["rrs0", "C"], writes=["pfb0"], tp=(0, 0))
                self.t_mm(self.PF[64:128, 0:512], C[64:65, 452:516], rrB, True,
                          True, reads=["rrs1", "C"], writes=["pfb0"],
                          tp=(64, 64))
                rBf = F[:, 2048:2560]
                self.ve("tensor_copy", rBf, pbc, reads=["pfb0"], writes=["rBf"])
                osl = O(hp)[:, qc * 512:(qc + 1) * 512]
                self.ve("tensor_tensor", osl[0:64, :],
                        self.pf(4, rows=(0, 64)), rBf[0:64, :], op=ALU.mult,
                        reads=["pfb4", "rBf"], writes=[kO])
                self.ve("tensor_tensor", osl[64:128, :],
                        self.pf(5, rows=(0, 64)), rBf[64:128, :], op=ALU.mult,
                        reads=["pfb5", "rBf"], writes=[kO])
        R.barrier()
        for w8 in range(8):
            sl = w8 % 2
            xw = F[:, sl * 2048:sl * 2048 + 2048]
            kxw = f"xw{sl}"
            self.dma("sync", _c3(xw), _xview(xsrc, w8 * 256, 256), reads=[xskey],
                     writes=[kxw])
            for ot in range(8):
                pbk = self.pf(ot % 2 + 1)
                kps = f"pfb{ot % 2 + 1}"
                for kt in range(8):
                    lhs = W1[:, 24576 + kt * 1024 + ot * 128:
                             24576 + kt * 1024 + ot * 128 + 128]
                    self.t_mm(pbk[:, 0:256], lhs,
                              O(kt)[:, w8 * 256:(w8 + 1) * 256],
                              kt == 0, kt == 7,
                              reads=[f"w1p{kt}", f"O{kt}"], writes=[kps])
                self.ve("scalar_tensor_tensor", xw[:, ot * 256:(ot + 1) * 256],
                        pbk[:, 0:256], C[:, 6 + ot:7 + ot],
                        xw[:, ot * 256:(ot + 1) * 256],
                        op0=ALU.add, op1=ALU.add,
                        reads=[kps, "C", kxw], writes=[kxw])
            self.dma("sync", _xview(xdst, w8 * 256, 256), _c3(xw), reads=[kxw],
                     writes=[xdkey])

    def _ffn(self, s, xsrc, xskey, xdst, xdkey):
        R, dma, F, C, W1, W2, A = (self.R, self.dma, self.F, self.C,
                                   self.W1, self.W2, self.A)
        R.barrier()
        for kt in range(8):
            dma("sync", W1[:, kt * 4096:(kt + 1) * 4096],
                self.WEXPBF[kt * 128:(kt + 1) * 128, :],
                reads=["WEXPBF"], writes=[f"w1e{kt}"])
        for g in range(32):
            dma("sync", W2[:, g * 1024:(g + 1) * 1024],
                self.WCONBF[g * 128:(g + 1) * 128, :],
                reads=["WCONBF"], writes=[f"w2c{g}"])
        xn = F[:, 5120:7168].bitcast(BF16)
        xn3 = xn.rearrange("p (ct t) -> p ct t", ct=8)
        hview = lambda mt: A[:, mt * 512:(mt + 1) * 512]
        for qc in range(4):
            for ch in (2 * qc, 2 * qc + 1):
                self._ln_chunk(xsrc, xskey, ch,
                               xn3[:, :, (ch % 2) * 256:(ch % 2) * 256 + 256],
                               f"fxn{ch % 2}")
            rkeys = ["fxn0", "fxn1"]
            for mt in range(32):
                pbk = self.pf(mt % 2)
                kps = f"pfb{mt % 2}"
                for kt in range(8):
                    lhs = W1[:, kt * 4096 + mt * 128:kt * 4096 + mt * 128 + 128]
                    self.t_mm(pbk, lhs, xn[:, kt * 512:(kt + 1) * 512],
                              kt == 0, kt == 7,
                              reads=[f"w1e{kt}"] + rkeys, writes=[kps])
                self.act(hview(mt), pbk, AF.Gelu, bias=C[:, 138 + mt:139 + mt],
                         reads=[kps, "C", "Cgb"], writes=[f"h{mt % 4}"])
            for og in range(2):
                for oi in range(4):
                    o = og * 4 + oi
                    pbk = self.pf(2 + oi)
                    kps = f"pfb{2 + oi}"
                    for g in range(32):
                        lhs = W2[:, g * 1024 + o * 128:g * 1024 + o * 128 + 128]
                        self.t_mm(pbk, lhs, hview(g), g == 0, g == 31,
                                  reads=[f"w2c{g}", f"h{g % 4}"], writes=[kps])
                    for half in range(2):
                        xcw = F[:, half * 2048:half * 2048 + 2048]
                        self.ve("scalar_tensor_tensor",
                                xcw[:, o * 256:(o + 1) * 256],
                                pbk[:, half * 256:half * 256 + 256],
                                C[:, 170 + o:171 + o],
                                xcw[:, o * 256:(o + 1) * 256],
                                op0=ALU.add, op1=ALU.add,
                                reads=[kps, "C", "Cb2", f"xc{half}"],
                                writes=[f"xc{half}"])
            for half in range(2):
                xcw = F[:, half * 2048:half * 2048 + 2048]
                self.dma("sync", _xview(xdst, (2 * qc + half) * 256, 256),
                         _c3(xcw), reads=[f"xc{half}"], writes=[xdkey])


    def _attn_b(self, s, xsrc, xskey, xdst, xdkey):
        R, dma, F, C, CB, RR, W1, W2, A = (self.R, self.dma, self.F, self.C,
                                           self.CB, self.RR, self.W1, self.W2,
                                           self.A)
        prevF = W2[:, 18432:22528].bitcast(F32)
        prevB = A[:, 16384:18432]
        Mslot = W2[:, 22528:26624]
        xdsl = lambda lh: W2[:, 26624 + lh * 2048:26624 + (lh + 1) * 2048]
        btsl = W2[:, 30720:30976]
        gmsl = lambda sh: W2[:, 30976 + sh * 256:30976 + (sh + 1) * 256]
        x4t = lambda j: W2[:, j * 1024:(j + 1) * 1024]
        bct = lambda j: W2[:, 16384 + j * 1024:16384 + (j + 1) * 1024]
        yf = lambda j: A[:, j * 1024:(j + 1) * 1024]
        yfall = A[:, 0:16384].rearrange("p (j t) -> p j t", j=16)
        xbcp = lambda j: A[:, j * 1027:(j + 1) * 1027]
        dtrow = RR[0:32, 0:1024]
        csrow = RR[64:96, 0:1024]
        dtcsT = F[:, 3072:3584]
        ecsT = F[:, 4608:4864]
        decT = F[:, 4864:4896]
        cdB = F[:, 4896:4928]
        decs = lambda lh: F[:, 4928 + lh * 32:4960 + lh * 32]
        Dsl = lambda i: F[:, i * 1024:(i + 1) * 1024]
        Esl = F[:, 4096:4608].bitcast(BF16)
        xn = F[:, 5120:7168].bitcast(BF16)
        xn3 = xn.rearrange("p (ct t) -> p ct t", ct=8)
        R.barrier()
        self.ve("memset", prevF, 0.0, reads=[], writes=["prevF"])
        for sp in range(2):
            R.barrier()
            tok0 = sp * 1024
            for kt in range(8):
                dma("sync", W1[:, kt * 2048:(kt + 1) * 2048],
                    self.WINBF[kt * 128:(kt + 1) * 128, 0:2048],
                    reads=["WINBF"], writes=[f"wz{kt}"])
            for kt in range(8):
                dma("sync", W1[:, 16384 + kt * 2336:16384 + (kt + 1) * 2336],
                    self.WINBF[kt * 128:(kt + 1) * 128, 2048:4384],
                    reads=["WINBF"], writes=[f"wx{kt}"])
            xbJ = A[:, 0:18486].rearrange("p (j c) -> p j c", c=1027)
            xbK = [f"xbcp{j}" for j in range(18)]
            if sp == 0:
                for j in range(18):
                    self.ve("memset", xbcp(j)[:, 0:3], 0.0, reads=[],
                            writes=[f"xbcp{j}"])
            else:
                dma("sync", xbJ[:, :, 0:3],
                    self.CVD.rearrange("p (j c) -> p j c", c=3),
                    reads=["CVD"], writes=xbK,
                    allow_slow_non_contiguous=True)
            for qc2 in range(2):
                for ch in (2 * qc2, 2 * qc2 + 1):
                    self._ln_chunk(xsrc, xskey, sp * 4 + ch,
                                   xn3[:, :, (ch % 2) * 256:(ch % 2) * 256 + 256],
                                   f"bxn{ch % 2}")
                rkeys = ["bxn0", "bxn1"]
                for mt in range(16):
                    pbk = self.pf(mt % 2)
                    kps = f"pfb{mt % 2}"
                    for kt in range(8):
                        lhs = W1[:, kt * 2048 + mt * 128:kt * 2048 + mt * 128 + 128]
                        self.t_mm(pbk, lhs, xn[:, kt * 512:(kt + 1) * 512],
                                  kt == 0, kt == 7,
                                  reads=[f"wz{kt}"] + rkeys, writes=[kps])
                    self.act(W2[:, mt * 512:(mt + 1) * 512], pbk, AF.Silu,
                             reads=[kps], writes=[f"sz{mt % 4}"])
                dma("scalar",
                    self.ZD[:, tok0 + qc2 * 512:tok0 + (qc2 + 1) * 512]
                    .rearrange("(mt p) t -> p mt t", p=128),
                    W2[:, 0:8192].rearrange("p (mt t) -> p mt t", mt=16),
                    reads=["sz0", "sz1", "sz2", "sz3"], writes=["ZD"])
                for j in range(18):
                    pbk = self.pf(2 + j % 2)
                    kps = f"pfb{2 + j % 2}"
                    for kt in range(8):
                        lhs = W1[:, 16384 + kt * 2336 + j * 128:
                                 16384 + kt * 2336 + j * 128 + 128]
                        self.t_mm(pbk, lhs, xn[:, kt * 512:(kt + 1) * 512],
                                  kt == 0, kt == 7,
                                  reads=[f"wx{kt}"] + rkeys, writes=[kps])
                    self.ve("tensor_copy",
                            xbcp(j)[:, 3 + qc2 * 512:3 + (qc2 + 1) * 512], pbk,
                            reads=[kps], writes=[f"xbcp{j}"])
                pbk = self.pf(4, rows=(0, 32))
                for kt in range(8):
                    lhs = W1[:, 16384 + kt * 2336 + 2304:16384 + kt * 2336 + 2336]
                    self.t_mm(pbk, lhs, xn[:, kt * 512:(kt + 1) * 512],
                              kt == 0, kt == 7,
                              reads=[f"wx{kt}"] + rkeys, writes=["pfb4"])
                self.ve("tensor_copy", dtrow[:, qc2 * 512:(qc2 + 1) * 512], pbk,
                        reads=["pfb4"], writes=["dtrow"])
            if sp == 0:
                dma("sync", self.CVD.rearrange("p (j c) -> p j c", c=3),
                    xbJ[:, :, 1024:1027], reads=xbK, writes=["CVD"],
                    allow_slow_non_contiguous=True)
            tr = self.PF[0:32, 2048:3072]
            self.act(tr, dtrow, AF.Abs, bias=C[0:32, 104:105],
                     reads=["dtrow", "C"], writes=["pfb4", "pfb5"])
            self.act(tr, tr, AF.Exp, scale=-1.0,
                     reads=["pfb4", "pfb5"], writes=["pfb4", "pfb5"])
            self.act(tr, tr, AF.Ln, bias=1.0,
                     reads=["pfb4", "pfb5"], writes=["pfb4", "pfb5"])
            self.ve("tensor_scalar", dtrow, dtrow, C[0:32, 104:105], 0.0,
                    op0=ALU.add, op1=ALU.max, reads=["dtrow", "C"],
                    writes=["dtrow"])
            self.ve("tensor_add", dtrow, dtrow, tr,
                    reads=["dtrow", "pfb4", "pfb5"], writes=["dtrow"])
            self.ve("tensor_scalar_mul", csrow, dtrow, C[0:32, 322:323],
                    reads=["dtrow", "C", "Cd"], writes=["csrow"])
            ones_b = C[64:96, 0:1].broadcast_to([32, 256])
            for z in range(4):
                self.ve("tensor_tensor_scan", csrow[:, z * 256:(z + 1) * 256],
                        ones_b, csrow[:, z * 256:(z + 1) * 256], 0.0,
                        op0=ALU.mult, op1=ALU.add,
                        reads=["csrow", "C"], writes=["csrow"])
            dma("scalar", self.CSD[:, tok0:tok0 + 1024], csrow,
                reads=["csrow"], writes=["CSD"])
            dma("scalar",
                self.CLD[sp * 4:(sp + 1) * 4, :].rearrange("z h -> h z"),
                csrow.rearrange("h (z t) -> h z t", z=4)[:, :, 255:256]
                .squeeze(2),
                reads=["csrow"], writes=["CLD"],
                allow_slow_non_contiguous=True)
            for c in range(8):
                pt_ = self.pf(5, 32)
                self.t_tr(pt_, dtrow[:, c * 128:(c + 1) * 128],
                          C[0:32, 192:224], reads=["dtrow", "C"],
                          writes=["pfb5"])
                self.ve("tensor_copy", dtcsT[:, c * 64:c * 64 + 32], pt_,
                        reads=["pfb5"], writes=["dtcsT"])
                pt2 = self.pf(5, 32, 64)
                self.t_tr(pt2, csrow[:, c * 128:(c + 1) * 128],
                          C[64:96, 256:288], reads=["csrow", "C"],
                          writes=["pfb5"])
                self.ve("tensor_copy", dtcsT[:, c * 64 + 32:c * 64 + 64], pt2,
                        reads=["pfb5"], writes=["dtcsT"])
            self.act(ecsT.rearrange("p (c h) -> p c h", c=8),
                     dtcsT.rearrange("p (c e) -> p c e", c=8)[:, :, 32:64],
                     AF.Exp, reads=["dtcsT"], writes=["ecsT"])
            acc1 = W2[:, 31488:32512]
            acc2 = Mslot[:, 0:1024]
            for j in range(18):
                cw = lambda k: C[:, 14 + k * 18 + j:15 + k * 18 + j]
                self.ve("tensor_scalar_mul", acc1, xbcp(j)[:, 3:1027], cw(3),
                        reads=[f"xbcp{j}", "C"], writes=["acc1"])
                self.ve("scalar_tensor_tensor", acc2, xbcp(j)[:, 2:1026], cw(2),
                        acc1, op0=ALU.mult, op1=ALU.add,
                        reads=[f"xbcp{j}", "C", "acc1"], writes=["acc2"])
                self.ve("scalar_tensor_tensor", acc1, xbcp(j)[:, 1:1025], cw(1),
                        acc2, op0=ALU.mult, op1=ALU.add,
                        reads=[f"xbcp{j}", "C", "acc2"], writes=["acc1"])
                self.ve("scalar_tensor_tensor", acc2, xbcp(j)[:, 0:1024], cw(0),
                        acc1, op0=ALU.mult, op1=ALU.add,
                        reads=[f"xbcp{j}", "C", "acc1"], writes=["acc2"])
                dst = x4t(j) if j < 16 else bct(j - 16)
                kdst = f"x4_{j}" if j < 16 else f"bc{j - 16}"
                wl = [kdst] + (["sz0", "sz1", "sz2", "sz3"] if j < 8 else [])
                self.act(dst, acc2, AF.Silu, bias=C[:, 86 + j:87 + j],
                         reads=["acc2", "C"], writes=wl)
            R.barrier()
            for z in range(4):
                zc0 = z * 256
                gc = 2 * z
                self.ve("tensor_copy", prevB, prevF, reads=["prevF"],
                        writes=["prevB"])
                for lh in range(2):
                    self.t_tr(self.pb16(0, 128, lh * 128),
                              bct(0)[:, zc0 + lh * 128:zc0 + (lh + 1) * 128],
                              CB[:, 0:128], reads=["bc0", "CB"], writes=["pbb0"])
                self.ve("tensor_copy", btsl, self.pb16(0, 256),
                        reads=["pbb0"], writes=["btsl"])
                for lh in range(2):
                    c = gc + lh
                    for hp in range(16):
                        bank = hp // 8
                        self.t_tr(self.pb16(bank, 128, (hp % 8) * 128),
                                  x4t(hp)[:, zc0 + lh * 128:zc0 + (lh + 1) * 128],
                                  CB[:, 0:128],
                                  reads=[f"x4_{hp}", "CB"], writes=[f"pbb{bank}"])
                    for bank in range(2):
                        h0 = bank * 16
                        mul = dtcsT[:, c * 64 + h0:c * 64 + h0 + 16] \
                            .unsqueeze(2).broadcast_to([128, 16, 64])
                        self.ve("tensor_tensor",
                                xdsl(lh)[:, bank * 1024:(bank + 1) * 1024]
                                .rearrange("p (hh d) -> p hh d", hh=16),
                                self.pb16(bank, 1024)
                                .rearrange("p (hh d) -> p hh d", hh=16),
                                mul, op=ALU.mult,
                                reads=[f"pbb{bank}", "dtcsT"],
                                writes=[f"xd{lh}"])
                for sh in range(2):
                    pg = self.pf(0, 256)
                    self.t_mm(pg, bct(0)[:, zc0 + sh * 128:zc0 + (sh + 1) * 128],
                              bct(1)[:, zc0:zc0 + 256], True, True,
                              reads=["bc0", "bc1"], writes=["pfb0"])
                    mask = CB[:, 132:388] if sh == 0 else CB[:, 388:644]
                    self.ve("tensor_tensor", gmsl(sh), pg, mask, op=ALU.mult,
                            reads=["pfb0", "CB"], writes=[f"gm{sh}"])
                clrow = F[0:1, 6144:6176]
                dma("sync", clrow, self.CLD[sp * 4 + z:sp * 4 + z + 1, :],
                    reads=["CLD"], writes=["clrow"])
                pcl = self.pf(1, 32)
                self.t_mm(pcl, C[0:1, 452:580], clrow, True, True,
                          reads=["clrow", "C"], writes=["pfb1"])
                self.act(cdB, pcl, AF.Exp, reads=["pfb1"], writes=["cdB"])
                for lh in range(2):
                    c = gc + lh
                    self.ve("tensor_tensor", decT, pcl,
                            dtcsT[:, c * 64 + 32:c * 64 + 64], op=ALU.subtract,
                            reads=["pfb1", "dtcsT"], writes=["decT"])
                    self.act(decs(lh), decT, AF.Exp, reads=["decT"],
                             writes=[f"decs{lh}"])
                for hb in range(4):
                    h0 = hb * 8
                    for half in range(2):
                        hh0 = h0 + half * 4
                        p0 = 0 if half == 0 else 64
                        zsl = F[p0:p0 + 1, 5120:6144]
                        kz = f"zst{half}"
                        dma("sync", zsl.rearrange("q (h t) -> q h t", h=4),
                            self.CSD[hh0:hh0 + 4, tok0 + zc0:tok0 + zc0 + 256]
                            .unsqueeze(0),
                            reads=["CSD"], writes=[kz])
                        orow = C[p0:p0 + 1, 452:580]
                        pe_ = self.PF[:, 0:1024]
                        ke_ = ["pfb0", "pfb1"]
                        self.t_mm(pe_[:, 0:512], orow, zsl[0:1, 0:512],
                                  True, True, reads=[kz, "C"], writes=ke_)
                        self.t_mm(pe_[:, 512:1024], orow, zsl[0:1, 512:1024],
                                  True, True, reads=[kz, "C"], writes=ke_)
                        csb = dtcsT[:, gc * 64 + 32 + hh0:gc * 64 + 36 + hh0] \
                            .unsqueeze(2).broadcast_to([128, 4, 256])
                        Dv = Dsl(half)
                        self.ve("tensor_tensor",
                                Dv.rearrange("p (hh t) -> p hh t", hh=4),
                                pe_.rearrange("p (hh t) -> p hh t", hh=4),
                                csb, op=ALU.subtract,
                                reads=ke_ + ["dtcsT"], writes=[f"Dv{half}"])
                        dv3 = Dv.rearrange("p (hh t) -> p hh t", hh=4)[:, :, 0:128]
                        self.ve("tensor_scalar_min", dv3, dv3, 0.0,
                                reads=[f"Dv{half}"], writes=[f"Dv{half}"],
                                eng="gpsimd")
                        self.act(Esl[:, 0:1024], Dv, AF.Exp,
                                 reads=[f"Dv{half}"], writes=["Esl"])
                        gmb = gmsl(0).unsqueeze(1).broadcast_to([128, 4, 256])
                        self.ve("tensor_tensor",
                                Mslot[:, half * 1024:(half + 1) * 1024]
                                .rearrange("p (hh t) -> p hh t", hh=4),
                                Esl[:, 0:1024]
                                .rearrange("p (hh t) -> p hh t", hh=4),
                                gmb, op=ALU.mult,
                                reads=["Esl", "gm0"], writes=["Ms0", "tmptok"])
                        zs1 = zsl.rearrange("q (hh t) -> q hh t", hh=4)[:, :, 128:256]
                        self.t_mm(pe_[:, 0:512]
                                  .rearrange("p (hh t) -> p hh t", hh=4),
                                  orow, zs1,
                                  True, True, reads=[kz, "C"], writes=ke_)
                        csb1 = dtcsT[:, (gc + 1) * 64 + 32 + hh0:
                                     (gc + 1) * 64 + 36 + hh0] \
                            .unsqueeze(2).broadcast_to([128, 4, 128])
                        self.ve("tensor_tensor",
                                Dv[:, 0:512].rearrange("p (hh t) -> p hh t", hh=4),
                                pe_[:, 0:512].rearrange("p (hh t) -> p hh t", hh=4),
                                csb1, op=ALU.subtract,
                                reads=ke_ + ["dtcsT"], writes=[f"Dv{half}"])
                        self.ve("tensor_scalar_min", Dv[:, 0:512], Dv[:, 0:512],
                                0.0, reads=[f"Dv{half}"], writes=[f"Dv{half}"],
                                eng="gpsimd")
                        self.act(Esl[:, 0:512], Dv[:, 0:512], AF.Exp,
                                 reads=[f"Dv{half}"], writes=["Esl"])
                        gmb1 = gmsl(1)[:, 128:256].unsqueeze(1) \
                            .broadcast_to([128, 4, 128])
                        self.ve("tensor_tensor",
                                Mslot[:, 2048 + half * 512:2048 + (half + 1) * 512]
                                .rearrange("p (hh t) -> p hh t", hh=4),
                                Esl[:, 0:512]
                                .rearrange("p (hh t) -> p hh t", hh=4),
                                gmb1, op=ALU.mult,
                                reads=["Esl", "gm1"], writes=["Ms1", "tmptok"])
                    for lp in range(4):
                        ghp = hb * 4 + lp
                        pyd = self.pf(2 + lp // 2, 256, (lp % 2) * 256)
                        kyd = f"pfb{2 + lp // 2}"
                        for hloc in range(2):
                            la = 2 * lp + hloc
                            xcol = ghp * 128 + hloc * 64
                            m0 = (la // 4) * 1024 + (la % 4) * 256
                            m1 = 2048 + (la // 4) * 512 + (la % 4) * 128
                            tp = (0, 0) if hloc == 0 else (0, 64)
                            outv = pyd if hloc == 0 else \
                                self.PF[64:128, (2 + lp // 2) * 512 +
                                        (lp % 2) * 256:(2 + lp // 2) * 512 +
                                        (lp % 2) * 256 + 256]
                            self.t_mm(outv if hloc == 1 else pyd[0:64, :],
                                      xdsl(0)[:, xcol:xcol + 64],
                                      Mslot[:, m0:m0 + 256],
                                      True, False, reads=["xd0", "Ms0"],
                                      writes=[kyd], tp=tp)
                            o2 = (pyd[0:64, 128:256] if hloc == 0 else
                                  self.PF[64:128, (2 + lp // 2) * 512 +
                                          (lp % 2) * 256 + 128:
                                          (2 + lp // 2) * 512 +
                                          (lp % 2) * 256 + 256])
                            self.t_mm(o2, xdsl(1)[:, xcol:xcol + 64],
                                      Mslot[:, m1:m1 + 128],
                                      False, True, reads=["xd1", "Ms1"],
                                      writes=[kyd], tp=tp)
                        self.ve("scalar_tensor_tensor",
                                yf(ghp)[:, zc0:zc0 + 256],
                                x4t(ghp)[:, zc0:zc0 + 256],
                                C[:, 106 + ghp:107 + ghp], pyd,
                                op0=ALU.mult, op1=ALU.add,
                                reads=[f"x4_{ghp}", "C", kyd], writes=["yfA"])
                for lh in range(2):
                    c = gc + lh
                    for colh in range(2):
                        pyo = self.PF[:, colh * 1024:(colh + 1) * 1024]
                        kyo = ["pfb0", "pfb1"] if colh == 0 else ["pfb2", "pfb3"]
                        for nn in range(2):
                            self.t_mm(pyo[:, nn * 512:(nn + 1) * 512],
                                      bct(1)[:, zc0 + lh * 128:zc0 + (lh + 1) * 128],
                                      prevB[:, colh * 1024 + nn * 512:
                                            colh * 1024 + (nn + 1) * 512],
                                      True, True, reads=["bc1", "prevB"],
                                      writes=kyo)
                        ecb = ecsT[:, c * 32 + colh * 16:c * 32 + colh * 16 + 16] \
                            .unsqueeze(2).broadcast_to([128, 16, 64])
                        self.ve("tensor_tensor",
                                Mslot[:, lh * 2048 + colh * 1024:
                                      lh * 2048 + (colh + 1) * 1024]
                                .rearrange("p (hh d) -> p hh d", hh=16),
                                pyo.rearrange("p (hh d) -> p hh d", hh=16),
                                ecb, op=ALU.mult,
                                reads=kyo + ["ecsT"],
                                writes=["tmptok", "Ms0", "Ms1"])
                for lh in range(2):
                    for hp in range(16):
                        bank = hp // 8
                        self.t_tr(self.pb16(bank, 128, (hp % 8) * 128),
                                  Mslot[:, lh * 2048 + hp * 128:
                                        lh * 2048 + (hp + 1) * 128],
                                  CB[:, 0:128],
                                  reads=["tmptok", "CB"], writes=[f"pbb{bank}"])
                    for bank in range(2):
                        j0 = bank * 8
                        dstv = yfall[:, j0:j0 + 8,
                                     zc0 + lh * 128:zc0 + (lh + 1) * 128]
                        self.ve("tensor_tensor", dstv, dstv,
                                self.pb16(bank, 1024)
                                .rearrange("p (j t) -> p j t", j=8),
                                op=ALU.add,
                                reads=["yfA", f"pbb{bank}"], writes=["yfA"])
                for lh in range(2):
                    db = decs(lh).unsqueeze(2).broadcast_to([128, 32, 64])
                    self.ve("tensor_tensor",
                            xdsl(lh).rearrange("p (hh d) -> p hh d", hh=32),
                            xdsl(lh).rearrange("p (hh d) -> p hh d", hh=32),
                            db, op=ALU.mult,
                            reads=[f"xd{lh}", f"decs{lh}"], writes=[f"xd{lh}"])
                for nn in range(4):
                    pst_ = self.pf(4 + nn % 2)
                    kst = f"pfb{4 + nn % 2}"
                    for lh in range(2):
                        self.t_mm(pst_, btsl[:, lh * 128:(lh + 1) * 128],
                                  xdsl(lh)[:, nn * 512:(nn + 1) * 512],
                                  lh == 0, lh == 1,
                                  reads=["btsl", f"xd{lh}"], writes=[kst])
                    cdb = cdB[:, nn * 8:nn * 8 + 8].unsqueeze(2) \
                        .broadcast_to([128, 8, 64])
                    tmp2 = Dsl(0)[:, 0:512]
                    self.ve("tensor_tensor",
                            tmp2.rearrange("p (hh d) -> p hh d", hh=8),
                            prevF[:, nn * 512:(nn + 1) * 512]
                            .rearrange("p (hh d) -> p hh d", hh=8),
                            cdb, op=ALU.mult,
                            reads=["prevF", "cdB"], writes=["Dv0"])
                    self.ve("tensor_tensor", prevF[:, nn * 512:(nn + 1) * 512],
                            tmp2, pst_, op=ALU.add,
                            reads=["Dv0", kst], writes=["prevF"])
            R.barrier()
            for kt in range(16):
                dma("sync", W1[:, kt * 1024:(kt + 1) * 1024],
                    self.WOUTBF[kt * 128:(kt + 1) * 128, :],
                    reads=["WOUTBF"], writes=[f"wo{kt}"])
            for qc2 in range(2):
                tq = tok0 + qc2 * 512
                sz = W2[:, 0:8192]
                dma("sync", sz.rearrange("p (mt t) -> p mt t", mt=16),
                    self.ZD[:, tq:tq + 512].rearrange("(mt p) t -> p mt t", p=128),
                    reads=["ZD"], writes=["szl"])
                for j in range(16):
                    self.ve("tensor_tensor", yf(j)[:, qc2 * 512:(qc2 + 1) * 512],
                            yf(j)[:, qc2 * 512:(qc2 + 1) * 512],
                            sz[:, j * 512:(j + 1) * 512], op=ALU.mult,
                            reads=["yfA", "szl"], writes=["yfA"])
                    self.ve("tensor_mul", W2[:, 8192 + (j % 4) * 512:
                                             8192 + (j % 4 + 1) * 512],
                            yf(j)[:, qc2 * 512:(qc2 + 1) * 512],
                            yf(j)[:, qc2 * 512:(qc2 + 1) * 512],
                            reads=["yfA"], writes=[f"t2_{j % 4}"], eng="gpsimd")
                    self.t_mm(self.pf(0, rows=(0, 1)), CB[:, 128:129],
                              W2[:, 8192 + (j % 4) * 512:8192 + (j % 4 + 1) * 512],
                              j == 0, j == 15,
                              reads=[f"t2_{j % 4}", "CB"], writes=["pfb0"])
                mrow = RR[0:1, 0:512]
                self.ve("tensor_scalar_mul", mrow, self.pf(0, rows=(0, 1)),
                        1.0 / 2048, reads=["pfb0"], writes=["mrow"])
                self.act(mrow, mrow, AF.Ln, bias=C[0:1, 1:2],
                         reads=["mrow", "C"], writes=["mrow"])
                self.act(mrow, mrow, AF.Exp, scale=-0.5, reads=["mrow"],
                         writes=["mrow"])
                prs = self.pf(1)
                self.t_mm(prs, C[0:1, 452:580], mrow, True, True,
                          reads=["mrow", "C"], writes=["pfb1"])
                rsb = F[:, 0:512]
                self.ve("tensor_copy", rsb, prs, reads=["pfb1"], writes=["rsb"])
                for j in range(16):
                    self.ve("tensor_tensor", yf(j)[:, qc2 * 512:(qc2 + 1) * 512],
                            yf(j)[:, qc2 * 512:(qc2 + 1) * 512], rsb,
                            op=ALU.mult, reads=["yfA", "rsb"], writes=["yfA"])
                for half in range(2):
                    w8 = (tq + half * 256 - 0) // 256
                    xcw = F[:, 512 + half * 2048:512 + half * 2048 + 2048]
                    kxw = f"bxw{half}"
                    self.dma("sync", _c3(xcw), _xview(xsrc, w8 * 256, 256),
                             reads=[xskey], writes=[kxw])
                for ot in range(8):
                    pbk = self.pf(2 + ot % 2)
                    kps = f"pfb{2 + ot % 2}"
                    for kt in range(16):
                        lhs = W1[:, kt * 1024 + ot * 128:kt * 1024 + ot * 128 + 128]
                        self.t_mm(pbk, lhs, yf(kt)[:, qc2 * 512:(qc2 + 1) * 512],
                                  kt == 0, kt == 15,
                                  reads=[f"wo{kt}", "yfA"], writes=[kps])
                    for half in range(2):
                        xcw = F[:, 512 + half * 2048:512 + half * 2048 + 2048]
                        kxw = f"bxw{half}"
                        self.ve("tensor_tensor",
                                xcw[:, ot * 256:(ot + 1) * 256],
                                pbk[:, half * 256:half * 256 + 256],
                                xcw[:, ot * 256:(ot + 1) * 256], op=ALU.add,
                                reads=[kps, kxw], writes=[kxw])
                for half in range(2):
                    w8 = (tq + half * 256) // 256
                    xcw = F[:, 512 + half * 2048:512 + half * 2048 + 2048]
                    kxw = f"bxw{half}"
                    self.dma("sync", _xview(xdst, w8 * 256, 256), _c3(xcw),
                             reads=[kxw], writes=[xdkey])
        R.barrier()

    def build(self):
        self._load_consts()
        self._setup_weights_bf16()
        if self.stage != "attn_a":
            self._setup_butterfly()
        for s in range(SPC):
            self._seq(s)
        self.R.finalize_and_replay(self.nc)
        self.ctx.close()
        return self.nc

    def _seq(self, s):
        nsub = {"attn_a": 1, "ffn1": 2, "attn_b": 3, "full": 4}[self.stage]
        dsts = []
        for i in range(nsub):
            if i == nsub - 1:
                dsts.append((self.ycore[s], f"ycore{s}"))
            else:
                dsts.append((self.XD[i % 2], f"XD{i % 2}_{s}"))
        self._attn(s, self.xcore[s], "xcore", dsts[0][0], dsts[0][1])
        self.R.barrier()
        if nsub >= 2:
            self._ffn(s, dsts[0][0], dsts[0][1], dsts[1][0], dsts[1][1])
            self.R.barrier()
        if nsub >= 3:
            self._attn_b(s, dsts[1][0], dsts[1][1], dsts[2][0], dsts[2][1])
            self.R.barrier()
        if nsub >= 4:
            self._ffn(s, dsts[2][0], dsts[2][1], dsts[3][0], dsts[3][1])
            self.R.barrier()


def kernel(**inputs):
    shared = _prep_shared(inputs)
    kb = KB(_STAGE)
    nc = kb.build()
    in_maps = []
    for c in range(N_CORES):
        m = dict(shared)
        m.update(_prep_core(inputs, c))
        in_maps.append(m)
    res = run_bass_kernel_spmd(nc, in_maps, core_ids=list(range(N_CORES)))
    out = np.empty((B, T, D), np.float32)
    for c in range(N_CORES):
        y = np.asarray(res.results[c]["ycore"], np.float32)
        out[c * SPC:(c + 1) * SPC] = y.transpose(0, 2, 1)
    return out
